# version 17
# baseline (speedup 1.0000x reference)
import numpy as np
import concourse.bass as bass
import concourse.mybir as mybir
from concourse.bass_utils import run_bass_kernel_spmd

F32 = mybir.dt.float32
BF16 = mybir.dt.bfloat16
OP = mybir.AluOpType
ACT = mybir.ActivationFunctionType

D = 1024
KT = 8
CH = 256
NCH = 16
NSEQ = 2
R = 16
M = 64
EPS = 1e-5
LR = 0.02
NCORE = 8


def build_core():
    nc = bass.Bass("TRN2", target_bir_lowering=False)

    dx = nc.dram_tensor("x", [NSEQ * NCH * CH, D], F32, kind="ExternalInput").ap()
    dout = nc.dram_tensor("out", [NSEQ * NCH * CH, D], F32, kind="ExternalOutput").ap()
    dwbT = nc.dram_tensor("wbT", [D, D], F32, kind="ExternalInput").ap()
    dwpT = nc.dram_tensor("wpT", [D, D], F32, kind="ExternalInput").ap()
    dwp = nc.dram_tensor("wp", [D, D], F32, kind="ExternalInput").ap()
    dinA = nc.dram_tensor("inA", [D, R], F32, kind="ExternalInput").ap()
    dinAT = nc.dram_tensor("inAT", [R, D], F32, kind="ExternalInput").ap()
    dinBT = nc.dram_tensor("inBT", [D, R], F32, kind="ExternalInput").ap()
    dgamma = nc.dram_tensor("gamma", [1, D], F32, kind="ExternalInput").ap()
    dbeta = nc.dram_tensor("beta", [1, D], F32, kind="ExternalInput").ap()
    dwin = nc.dram_tensor("win", [D, M], F32, kind="ExternalInput").ap()
    dbinc = nc.dram_tensor("binc", [M, 1], F32, kind="ExternalInput").ap()
    db2c = nc.dram_tensor("b2c", [M, 1], F32, kind="ExternalInput").ap()
    dbdext = nc.dram_tensor("bdext", [M + 1, D], F32, kind="ExternalInput").ap()
    dbd2 = nc.dram_tensor("bd2", [128, 8 * M], F32, kind="ExternalInput").ap()
    dwoutext = nc.dram_tensor("woutext", [M + 1, D], F32, kind="ExternalInput").ap()
    dcvecT = nc.dram_tensor("cvecT", [D, NSEQ], F32, kind="ExternalInput").ap()
    dwmod = nc.dram_tensor("wmod", [D, 6 * D], F32, kind="ExternalInput").ap()
    dmodp = nc.dram_tensor("modp", [NSEQ, 6 * D], F32).ap()

    def sb(name, shape, dt=BF16, tr=False):
        return nc.alloc_sbuf_tensor(name, shape, dt, is_transpose_dest=tr).ap()

    wbT_sb = sb("wbT_sb", [128, KT * D])
    wpT_sb = sb("wpT_sb", [128, KT * D])
    win_sb = sb("win_sb", [128, KT * M])
    inBT_sb = sb("inBT_sb", [128, KT * R])
    inA_sb = sb("inA_sb", [128, KT * R])
    wpa_sb = sb("wpa_sb", [128, KT * R], tr=True)
    bdext_sb = sb("bdext_sb", [M + 1, D])
    bd2_sb = sb("bd2_sb", [128, 8 * M])
    woutext_sb = sb("woutext_sb", [M + 1, D])
    a2t = sb("a2t", [R, D])
    binc_sb = sb("binc_sb", [M, 1], F32)
    b2c_sb = sb("b2c_sb", [M, 1], F32)
    gcolf = sb("gcolf", [128, KT], F32)
    bcolb = sb("bcolb", [128, KT])
    biasV = sb("biasV", [1, D])
    negbV = sb("negbV", [1, D])
    biasP = sb("biasP", [1, R])
    negbP = sb("negbP", [1, R])
    ones1 = sb("ones1", [1, 128])
    oneh = sb("oneh", [1, 128])
    ones128 = sb("ones128", [128, 128])
    F32R = mybir.dt.float32r
    wmR = sb("wmR", [128, KT * 512], F32R)
    silur = sb("silur", [128, KT * NSEQ], F32R)
    silucT = sb("silucT", [128, KT * NSEQ], F32)

    reps = [{g: sb(f"rep_{g}_{s}", [128, D]) for g in
             ("sa", "ca1", "ga", "sb_", "cb1", "gb")} for s in range(NSEQ)]

    xbuf = [sb(f"x_tm{i}", [128, 2 * D], F32) for i in range(3)]
    hbuf = [sb(f"h_tm{i}", [128, 2 * D]) for i in range(2)]
    d2buf = [sb(f"d2_tm{i}", [128, 2 * D]) for i in range(2)]
    hfm = [sb(f"hfm{i}", [128, 2 * D], tr=True) for i in range(2)]
    xnfmc = [sb(f"xnfmc{i}", [128, 2 * D], tr=True) for i in range(2)]
    xnfm = [sb(f"xnfm{i}", [128, KT * 257]) for i in range(2)]
    h2_tm = sb("h2_tm", [128, 2 * D])
    h2fm = sb("h2fm", [128, 2 * D], tr=True)
    v_sb = sb("v_sb", [128, 2 * D])
    pin_sb = sb("pin_sb", [128, 2 * R])
    perr_sb = sb("perr_sb", [128, 2 * R])
    midT_sb = sb("midT_sb", [R, CH])
    t_sb = sb("t_sb", [128, 2 * D])
    sext = sb("sext", [M + 1, CH])
    gext = sb("gext", [M + 1, CH])
    x2_tm = sb("x2_tm", [128, 2 * D], F32)
    wouteff = sb("wouteff", [M + 1, D])
    obuf = [sb(f"out_tm{i}", [128, 2 * D], F32) for i in range(2)]
    wpaT_sb = t_sb[0:R, 0:D]
    cvec_sb = obuf[1][0:NSEQ, 0:D]
    modtmp = obuf[1][0:NSEQ, D:2 * D]
    modt2 = obuf[0][0:NSEQ, 0:D]
    modt3 = obuf[0][0:NSEQ, D:2 * D]
    junk_v = sb("junk_v", [128, D])
    junk_g = sb("junk_g", [128, D])
    junk_s = sb("junk_s", [128, D])
    cst = sb("cst", [128, 4], F32)
    stat = [sb(f"stat{i}", [128, 24], F32) for i in range(2)]
    stat2 = [sb(f"stat2_{i}", [128, 16], F32) for i in range(2)]
    statn = sb("statn", [128, 2], F32)
    statnb = sb("statnb", [128, 2])
    coefb = sb("coefb", [128, 6], F32)
    accAT = sb("accAT", [R, D], F32)
    accBT = sb("accBT", [128, 128], F32)
    aeff = sb("aeff", [R, D])
    beffT = sb("beffT", [128, 128])

    pbig = nc.alloc_psum_tensor("pbig", [128, 2048], F32).ap()
    psm = nc.alloc_psum_tensor("psm", [128, 512], F32).ap()
    pda = nc.alloc_psum_tensor("pda", [128, 1024], F32).ap()
    pdb = nc.alloc_psum_tensor("pdb", [128, 128], F32).ap()

    def v3(ap, inner):
        return ap.rearrange("p (a b) -> p a b", b=inner)

    wbT_v = v3(wbT_sb[:, :], D)
    wpT_v = v3(wpT_sb[:, :], D)
    win_v = v3(win_sb[:, :], M)
    inBT_v = v3(inBT_sb[:, :], R)
    inA_v = v3(inA_sb[:, :], R)
    wpa_v = v3(wpa_sb[:, :], R)
    wmR_v = v3(wmR[:, :], 512)
    silur_v = v3(silur[:, :], NSEQ)
    bd2_v = v3(bd2_sb[:, :], M)

    sem = {k: nc.alloc_semaphore(k) for k in ("t", "v", "s", "g", "dG", "dY", "dQ", "x0", "x1", "x2", "q10", "q11", "q20", "q21", "q30", "q31", "st0", "st1")}
    cnt = {k: 0 for k in sem}
    EV = {}
    prog = {k: [] for k in ("tensor", "vector", "scalar", "sync", "gpsimd")}
    ENGKEY = {"tensor": "t", "vector": "v", "scalar": "s",
              "sync": "dY", "gpsimd": "g"}

    def op(eng, fn, ev=None):
        key = ENGKEY[eng]
        val = None
        if ev is not None:
            cnt[key] += 1
            val = cnt[key]
            EV[ev] = (key, val)
        h = sem[key]

        def run(e, fn=fn, val=val):
            inst = fn(e)
            if val is not None:
                inst.then_inc(h, 1)
        prog[eng].append(run)

    def wt(eng, ev):
        if ev not in EV:
            return
        key, val = EV[ev]
        h = sem[key]
        prog[eng].append(lambda e, h=h, val=val: e.wait_ge(h, val))

    def dma(eng, ring, out, in_, ev=None, transpose=False, slow=False):
        cnt[ring] += 16
        val = cnt[ring]
        if ev is not None:
            EV[ev] = (ring, val)
        h = sem[ring]

        def run(e, out=out, in_=in_, transpose=transpose, slow=slow):
            e.dma_start(out, in_, transpose=transpose,
                        allow_slow_non_contiguous=slow).then_inc(h, 16)
        prog[eng].append(run)

    def waitval(eng, ring, val):
        h = sem[ring]
        prog[eng].append(lambda e, h=h, val=val: e.wait_ge(h, val))

    def tiled_load(name_ev, dst_v, src, cols, nt=KT, slow=False):
        dma("gpsimd", "dG", dst_v,
            src.rearrange("(t p) c -> p t c", p=128), ev=name_ev, slow=slow)

    wp_stage = [v3(xnfmc[0][:, :], 512), v3(xnfmc[1][:, :], 512)]

    def wp_half_load(half, ev):
        for grp in range(2):
            dma("gpsimd", "dG", wp_stage[grp][:, 0:4, :],
                dwp[512 * grp:512 * (grp + 1),
                    512 * half:512 * (half + 1)].rearrange("(t p) c -> p t c", p=128),
                ev=(ev if grp == 1 else None))
    tiled_load("L_wpT", wpT_v[:, :, :], dwpT[:, :], D)
    tiled_load("L_wbT", wbT_v[:, :, :], dwbT[:, :], D)
    tiled_load("L_win", win_v[:, :, :], dwin[:, :], M)
    tiled_load("L_inBT", inBT_v[:, :, :], dinBT[:, :], R)
    tiled_load("L_inA", inA_v[:, :, :], dinA[:, :], R)
    dma("gpsimd", "dG", bdext_sb[:, :], dbdext[:, :], ev="L_bdext")
    dma("gpsimd", "dG", bd2_sb[:, :], dbd2[:, :], ev="L_bd2")
    dma("gpsimd", "dG", woutext_sb[:, :], dwoutext[:, :], ev="L_woutext")
    dma("gpsimd", "dG", a2t[:, :], dinAT[:, :], ev="L_inAT")
    dma("gpsimd", "dG", binc_sb[:, :], dbinc[:, :], ev="L_binc")
    dma("gpsimd", "dG", b2c_sb[:, :], db2c[:, :], ev="L_b2c")
    dma("gpsimd", "dG", gcolf[:, :],
        bass.AP(dgamma.tensor, 0, [[1, 128], [128, KT]]), ev="L_gcol", slow=True)
    dma("gpsimd", "dG", bcolb[:, :],
        bass.AP(dbeta.tensor, 0, [[1, 128], [128, KT]]), ev="L_bcol", slow=True)
    tiled_load("L_cvt", v3(silucT[:, :], NSEQ)[:, :, :], dcvecT[:, :], NSEQ, slow=True)
    waitval("gpsimd", "dG", cnt["dG"])
    op("gpsimd", lambda e: e.memset(junk_g[0:1, 0:1], 0.0), ev="gLD")


    CONSTS = (EPS, LR * LR, 1.0 / D, -1.0)
    for j, val in enumerate(CONSTS):
        op("vector", lambda e, j=j, val=val: e.memset(cst[:, j:j + 1], val))
        nc.const_aps.register(F32, val, cst[:, j:j + 1])
    op("vector", lambda e: e.memset(sext[M:M + 1, :], 1.0))
    op("vector", lambda e: e.memset(gext[M:M + 1, :], 1.0))
    op("vector", lambda e: e.memset(ones1[:, :], 1.0))
    op("vector", lambda e: e.memset(oneh[:, :], 0.0), ev="v_oh0")
    wt("vector", "v_oh0")
    op("vector", lambda e: e.memset(oneh[0:1, 127:128], 1.0))
    op("vector", lambda e: e.memset(ones128[:, :], 1.0))
    op("vector", lambda e: e.memset(accAT[:, :], 0.0))
    op("vector", lambda e: e.memset(accBT[:, :], 0.0), ev="vRST0")
    wt("vector", "gLD")
    op("vector", lambda e: e.tensor_scalar(a2t[:, :], a2t[:, :], 2.0, None, OP.mult),
       ev="v_a2t")

    wt("tensor", "gLD")
    for half in range(2):
        if half == 1:
            wt("gpsimd", "t_wpah0")
        wp_half_load(half, f"L_wph{half}")
        wt("tensor", f"L_wph{half}")

        def wpa_mms(e, half=half):
            last = None
            for ht in range(KT):
                last = e.matmul(pda[0:R, 512 * half:512 * (half + 1)],
                                inA_v[:, ht, :], wp_stage[ht // 4][:, ht % 4, :],
                                start=(ht == 0), stop=(ht == KT - 1),
                                skip_group_check=True)
            return last
        op("tensor", wpa_mms, ev=("t_wpah0" if half == 0 else "t_wpa"))

    wt("vector", "t_wpa")
    op("vector", lambda e: e.tensor_copy(wpaT_sb[:, :], pda[0:R, 0:D]), ev="v_wpaT")

    wt("scalar", "v_wpaT")
    dma("scalar", "dQ", v3(wpa_sb[:, :], R)[:, :, :], wpaT_sb[:, :],
        ev="T_wpa", transpose=True)

    wt("tensor", "gLD")
    wt("tensor", "gLD")
    wt("tensor", "t_wpa")
    wt("tensor", "v_wpaT")

    def bias_mms(e):
        last = None
        for half in range(2):
            for kt in range(KT):
                last = e.matmul(pda[0:1, 512 * half + 0:512 * (half + 1)],
                                bcolb[:, kt:kt + 1],
                                wpT_v[:, kt, 512 * half:512 * (half + 1)],
                                start=(kt == 0), stop=(kt == KT - 1),
                                skip_group_check=True)
        return last
    op("tensor", bias_mms, ev="t_biasV")
    wt("tensor", "T_wpa")

    def biasp_mms(e):
        last = None
        for kt in range(KT):
            last = e.matmul(psm[0:1, 0:R], bcolb[:, kt:kt + 1], wpa_v[:, kt, :],
                            start=(kt == 0), stop=(kt == KT - 1),
                            skip_group_check=True)
        return last
    op("tensor", biasp_mms, ev="t_biasP")

    wt("vector", "t_biasV")
    op("vector", lambda e: e.tensor_copy(biasV[:, :], pda[0:1, 0:D]))
    op("vector", lambda e: e.tensor_scalar(negbV[:, :], biasV[:, :], -1.0, None, OP.mult))
    wt("vector", "t_biasP")
    op("vector", lambda e: e.tensor_copy(biasP[:, :], psm[0:1, 0:R]), ev="v_bp0")
    wt("vector", "v_bp0")
    op("vector", lambda e: e.tensor_scalar(negbP[:, :], biasP[:, :], -1.0, None, OP.mult),
       ev="v_bias")

    wt("vector", "gLD")
    wt("vector", "t_biasV")
    for kt in range(KT):
        op("vector", lambda e, kt=kt: e.tensor_scalar(
            wpT_v[:, kt, :], wpT_v[:, kt, :], gcolf[:, kt:kt + 1], None, OP.mult))
    wt("vector", "t_biasP")
    for kt in range(KT):
        op("vector", lambda e, kt=kt: e.tensor_scalar(
            wpa_v[:, kt, :], wpa_v[:, kt, :], gcolf[:, kt:kt + 1], None, OP.mult),
            ev=("v_wfold" if kt == KT - 1 else None))

    wt("scalar", "gLD")
    op("scalar", lambda e: e.activation(silur[:, :], silucT[:, :], ACT.Silu),
       ev="s_silu")

    for g in range(6):
        for hf in range(2):
            if not (g == 0 and hf == 0):
                wt("gpsimd", f"t_mod{g}_0" if hf == 1 else f"t_mod{g - 1}")
            dma("gpsimd", "dG", wmR_v[:, :, :],
                dwmod[:, 1024 * g + 512 * hf:1024 * g + 512 * (hf + 1)]
                .rearrange("(t p) c -> p t c", p=128),
                ev=f"L_wm{g}_{hf}")
            wt("tensor", f"L_wm{g}_{hf}")
            if g == 0 and hf == 0:
                wt("tensor", "s_silu")
                wt("tensor", "t_biasP")
                wt("tensor", "v_bias")

            def mod_mms(e, hf=hf):
                last = None
                for kt in range(KT):
                    last = e.matmul(pda[0:NSEQ, 512 * hf:512 * (hf + 1)],
                                    silur_v[:, kt, :], wmR_v[:, kt, :],
                                    start=(kt == 0), stop=(kt == KT - 1),
                                    skip_group_check=True)
                return last
            op("tensor", mod_mms, ev=(f"t_mod{g}" if hf == 1 else f"t_mod{g}_0"))

        wt("vector", f"t_mod{g}")
        if g > 0:
            wt("vector", f"D_modst{g - 1}")
        op("vector", lambda e: e.tensor_copy(modtmp[:, :], pda[0:NSEQ, 0:D]),
           ev=f"v_modev{g}")
        if g in (1, 4):
            op("vector", lambda e: e.tensor_scalar(
                modtmp[:, :], modtmp[:, :], 1.0, None, OP.add), ev=f"v_modp{g}")
        elif g in (2, 5):
            wt("scalar", f"v_modev{g}")
            op("scalar", lambda e: e.sign(modt2[:, :], modtmp[:, :]))
            op("scalar", lambda e: e.activation(modt3[:, :], modtmp[:, :], ACT.Abs),
               ev=f"s_modg{g}")
            wt("vector", f"s_modg{g}")
            op("vector", lambda e: e.tensor_tensor(
                junk_v[0:NSEQ, 0:D], modt2[:, :], modt2[:, :], OP.mult))
            op("vector", lambda e: e.tensor_scalar(
                junk_v[0:NSEQ, 0:D], junk_v[0:NSEQ, 0:D], -1.0, 1.0, OP.mult, OP.add))
            op("vector", lambda e: e.tensor_add(
                modt2[:, :], modt2[:, :], junk_v[0:NSEQ, 0:D]))
            op("vector", lambda e: e.tensor_single_scalar(
                modt3[:, :], modt3[:, :], 0.25, OP.max))
            op("vector", lambda e: e.tensor_mul(
                modtmp[:, :], modt2[:, :], modt3[:, :]), ev=f"v_modp{g}")
        else:
            op("vector", lambda e: e.engine_nop(), ev=f"v_modp{g}")
        wt("gpsimd", f"v_modp{g}")
        dma("gpsimd", "dG", dmodp[:, 1024 * g:1024 * (g + 1)], modtmp[:, :],
            ev=f"D_modst{g}")

    waitval("gpsimd", "dG", cnt["dG"])
    GORDER = ("sa", "ca1", "ga", "sb_", "cb1", "gb")
    for s in range(NSEQ):
        for g in range(6):
            wt("gpsimd", f"D_modst{g}")
            dma("gpsimd", "dG", reps[s][GORDER[g]][:, :],
                bass.AP(dmodp.tensor, s * 6 * D + 1024 * g, [[0, 128], [1, D]]),
                ev=f"D_bc{s}_{g}")

    wt("vector", f"D_bc{NSEQ - 1}_5")
    wt("vector", "L_woutext")
    op("vector", lambda e: e.tensor_tensor(
        wouteff[:, :], woutext_sb[:, :], reps[0]["gb"][0:M + 1, :], OP.mult),
        ev="vWOF0")

    for eng in ("tensor", "vector", "scalar", "sync"):
        for ev in ("t_mod5", "v_modp5", "s_modg5", f"D_bc{NSEQ - 1}_5", "v_wfold",
                   "v_a2t", "v_bias"):
            wt(eng, ev)

    def A_load(gc):
        s, c = divmod(gc, NCH)
        x3 = gc % 3
        xq = xbuf[x3]
        wt("gpsimd", f"vX2_{gc - 3}")
        rows = (s * NCH + c) * CH
        dma("gpsimd", f"x{gc % 3}",
            v3(xq[:, :], D)[:, :, :],
            dx[rows:rows + CH, :].rearrange("(a p) c -> p a c", p=128),
            ev=f"Dx_{gc}")

    def A_ln(gc):
        s, c = divmod(gc, NCH)
        q = gc % 2
        x3 = gc % 3
        xq, hq, d2q, hfq, xnq = xbuf[x3], hbuf[q], d2buf[q], hfm[q], xnfm[q]
        xncq = xnfmc[q]
        st = stat[q]
        rep = reps[s]

        waitval("vector", f"x{gc % 3}", EV[f"Dx_{gc}"][1])
        for ct in range(2):
            op("vector", lambda e, ct=ct: e.tensor_scalar(
                junk_v[:, :], xq[:, D * ct:D * (ct + 1)], 1.0 / D, None,
                OP.mult, OP.add, accum_out=st[:, 4 + ct:5 + ct]),
                ev=(f"vS1_{gc}" if ct == 1 else None))

        waitval("scalar", f"x{gc % 3}", EV[f"Dx_{gc}"][1])
        wt("scalar", f"vD2_{gc - 2}")
        for ct in range(2):
            op("scalar", lambda e, ct=ct: e.activation(
                junk_s[:, :], xq[:, D * ct:D * (ct + 1)], ACT.Square,
                scale=1.0 / 32.0, accum_out=st[:, 16 + ct:17 + ct]))
        wt("scalar", f"vS1_{gc}")
        _ln_algebra(st, None, 4, 16, f"sS1_{gc}")

        wt("vector", f"sS1_{gc}")
        op("vector", lambda e: e.reciprocal(st[:, 6:8], st[:, 16:18]),
           ev=f"vR1_{gc}")
        wt("vector", f"vR1_{gc}")
        wt("vector", f"tDB_{gc - 2}")
        wt("vector", f"T1_{gc - 2}")
        for ct in range(2):
            op("vector", lambda e, ct=ct: e.scalar_tensor_tensor(
                hq[:, D * ct:D * (ct + 1)], xq[:, D * ct:D * (ct + 1)],
                st[:, 4 + ct:5 + ct], rep["ca1"][:, :], OP.subtract, OP.mult))
        for ct in range(2):
            op("vector", lambda e, ct=ct: e.scalar_tensor_tensor(
                hq[:, D * ct:D * (ct + 1)], hq[:, D * ct:D * (ct + 1)],
                st[:, 6 + ct:7 + ct], rep["sa"][:, :], OP.mult, OP.add,
                accum_out=st[:, 8 + ct:9 + ct]),
                ev=(f"vH_{gc}" if ct == 1 else None))

        wt("scalar", f"vH_{gc}")
        for ct in range(2):
            op("scalar", lambda e, ct=ct: e.activation(
                junk_s[:, :], hq[:, D * ct:D * (ct + 1)], ACT.Square,
                scale=1.0 / 32.0, accum_out=st[:, 18 + ct:19 + ct]))
        _ln_algebra(st, 8, 12, 18, f"sS2_{gc}")

    def A_tr(gc):
        s, c = divmod(gc, NCH)
        q = gc % 2
        x3 = gc % 3
        xq, hq, d2q, hfq, xnq = xbuf[x3], hbuf[q], d2buf[q], hfm[q], xnfm[q]
        xncq = xnfmc[q]
        st = stat[q]

        wt("vector", f"sS2_{gc}")
        op("vector", lambda e: e.reciprocal(st[:, 14:16], st[:, 18:20]),
           ev=f"vR2_{gc}")
        wt("vector", f"vR2_{gc}")
        wt("vector", f"T2_{gc - 2}")
        for ct in range(2):
            op("vector", lambda e, ct=ct: e.tensor_scalar(
                d2q[:, D * ct:D * (ct + 1)], hq[:, D * ct:D * (ct + 1)],
                st[:, 12 + ct:13 + ct], st[:, 14 + ct:15 + ct],
                OP.subtract, OP.mult),
                ev=(f"vD2_{gc}" if ct == 1 else None))

        wt("sync", f"vD2_{gc}")
        wt("sync", f"vXN_{gc - 2}")
        wt("sync", f"vBND_{gc - 3}")
        wt("sync", f"tVc0_{gc - 2}")
        for ct in range(2):
            dma("sync", f"q2{gc % 2}",
                v3(xncq[:, :], CH)[:, :, 128 * ct:128 * (ct + 1)],
                d2q[:, D * ct:D * (ct + 1)], transpose=True,
                ev=(f"T2_{gc}" if ct == 1 else None))
        wt("sync", f"vH_{gc}")
        wt("sync", f"tBMM_{gc - 2}")
        for ct in range(2):
            dma("sync", f"q1{gc % 2}",
                v3(hfq[:, :], CH)[:, :, 128 * ct:128 * (ct + 1)],
                hq[:, D * ct:D * (ct + 1)], transpose=True,
                ev=(f"T1_{gc}" if ct == 1 else None))

        waitval("gpsimd", f"q2{gc % 2}", EV[f"T2_{gc}"][1])
        wt("gpsimd", f"tVMM_{gc - 2}")
        op("gpsimd", lambda e: e.tensor_copy(
            v3(xnq[:, :], 257)[:, :, 128:256], v3(xncq[:, :], CH)[:, :, 128:256]),
            ev=f"vXN_{gc}")
        if c > 0:
            prev = xnfm[(gc - 1) % 2]
            op("gpsimd", lambda e: e.tensor_copy(
                v3(prev[:, :], 257)[:, :, 256:257],
                v3(xncq[:, :], CH)[:, :, 0:1]), ev=f"vBND_{gc - 1}")
        if c == NCH - 1:
            op("gpsimd", lambda e: e.memset(
                v3(xnq[:, :], 257)[:, :, 256:257], 0.0), ev=f"vBND_{gc}")

    _fence_n = [0]

    def _ln_algebra(st, sumc, muc, e2c, ev):
        _fence_n[0] += 1
        f = f"sF{_fence_n[0]}"
        tm_ = e2c + 4
        if sumc is not None:
            op("scalar", lambda e: e.mul(st[:, muc:muc + 2],
                                         st[:, sumc:sumc + 2], 1.0 / D), ev=f + "a")
            wt("scalar", f + "a")
        op("scalar", lambda e: e.activation(st[:, tm_:tm_ + 2],
                                            st[:, muc:muc + 2], ACT.Square), ev=f + "b")
        wt("scalar", f + "b")
        op("scalar", lambda e: e.mul(st[:, tm_:tm_ + 2],
                                     st[:, tm_:tm_ + 2], -1.0), ev=f + "c")
        wt("scalar", f + "c")
        for j in range(2):
            op("scalar", lambda e, j=j: e.add(st[:, e2c + j:e2c + j + 1],
                                              st[:, e2c + j:e2c + j + 1],
                                              st[:, tm_ + j:tm_ + j + 1]),
               ev=(f + "d" if j == 1 else None))
        wt("scalar", f + "d")
        op("scalar", lambda e: e.activation(st[:, e2c:e2c + 2], st[:, e2c:e2c + 2],
                                            ACT.Sqrt, bias=EPS), ev=ev)

    def B_V(gc):
        s, c = divmod(gc, NCH)
        q = gc % 2
        xnv = v3(xnfm[q][:, :], 257)
        last_chunk = (c == NCH - 1)

        wt("tensor", f"vX2_{gc - 1}")
        wt("tensor", f"sGEV_{gc - 2}")
        wt("tensor", "v_wfold")
        wt("tensor", "v_a2t")

        xncv = v3(xnfmc[q][:, :], CH)

        def v_group_ct(e, ct, lh_of):
            last = None
            for kt in range(KT):
                lh = lh_of(kt)
                e.matmul(pbig[:, 1024 * ct + 0:1024 * ct + 512], lh,
                         wpT_v[:, kt, 0:512],
                         start=(kt == 0), stop=False, skip_group_check=True)
                e.matmul(pbig[:, 1024 * ct + 512:1024 * ct + 1024], lh,
                         wpT_v[:, kt, 512:1024],
                         start=(kt == 0), stop=False, skip_group_check=True)
                e.matmul(psm[:, R * ct:R * (ct + 1)], lh, wpa_v[:, kt, :],
                         start=(kt == 0), stop=False, skip_group_check=True)
            patch = last_chunk and ct == 1
            e.matmul(pbig[:, 1024 * ct + 0:1024 * ct + 512], ones1[:, :],
                     biasV[0:1, 0:512], start=False, stop=not patch,
                     skip_group_check=True)
            e.matmul(pbig[:, 1024 * ct + 512:1024 * ct + 1024], ones1[:, :],
                     biasV[0:1, 512:1024], start=False, stop=not patch,
                     skip_group_check=True)
            last = e.matmul(psm[:, R * ct:R * (ct + 1)], ones1[:, :],
                            biasP[0:1, :], start=False, stop=not patch,
                            skip_group_check=True)
            if patch:
                e.matmul(pbig[:, 1024 * ct + 0:1024 * ct + 512], oneh[:, :],
                         negbV[0:1, 0:512], start=False, stop=True,
                         skip_group_check=True)
                e.matmul(pbig[:, 1024 * ct + 512:1024 * ct + 1024], oneh[:, :],
                         negbV[0:1, 512:1024], start=False, stop=True,
                         skip_group_check=True)
                last = e.matmul(psm[:, R * ct:R * (ct + 1)], oneh[:, :],
                                negbP[0:1, :], start=False, stop=True,
                                skip_group_check=True)
            return last
        waitval("tensor", f"q2{gc % 2}", EV[f"T2_{gc}"][1])
        op("tensor", lambda e: v_group_ct(e, 0, lambda kt: xncv[:, kt, 1:129]),
           ev=f"tVc0_{gc}")
        wt("tensor", f"vXN_{gc}")
        wt("tensor", f"vBND_{gc}")
        op("tensor", lambda e: v_group_ct(e, 1, lambda kt: xnv[:, kt, 129:257]),
           ev=f"tVMM_{gc}")

        wt("vector", f"tVMM_{gc}")
        op("vector", lambda e: e.tensor_copy(perr_sb[:, :], psm[:, 0:2 * R]),
           ev=f"vPE_{gc}")

        wt("tensor", f"gNRM_{gc}")
        wt("tensor", f"vACC_{gc - 1}")

        def norm_mms(e):
            e.matmul(pda[:, 0:1], ones128[0:R, :], statnb[0:R, 0:1],
                     start=True, stop=False, skip_group_check=True)
            return e.matmul(pda[:, 0:1], ones128[:, :], statnb[:, 1:2],
                            start=False, stop=True, skip_group_check=True)
        op("tensor", norm_mms, ev=f"tNRM_{gc}")

        wt("tensor", f"vPE_{gc}")
        hq_ = hbuf[q]

        def db_mms(e):
            last = None
            for kk in range(KT):
                for ct in range(2):
                    last = e.matmul(pdb[:, R * kk:R * (kk + 1)],
                                    hq_[:, 1024 * ct + 128 * kk:1024 * ct + 128 * (kk + 1)],
                                    perr_sb[:, R * ct:R * (ct + 1)],
                                    start=(ct == 0), stop=(ct == 1),
                                    skip_group_check=True)
            return last
        op("tensor", db_mms, ev=f"tDB_{gc}")

        wt("scalar", f"tVMM_{gc}")
        op("scalar", lambda e: e.copy(v_sb[:, 0:D], pbig[:, 0:D]))
        op("scalar", lambda e: e.copy(v_sb[:, D:2 * D], pbig[:, D:2 * D]),
           ev=f"sVEV_{gc}")
        wt("scalar", f"tNRM_{gc}")
        op("scalar", lambda e: e.activation(coefb[:, 0:1], pda[:, 0:1], ACT.Sqrt,
                                            scale=LR * LR), ev=f"sSD_{gc}")

        wt("vector", f"sSD_{gc}")
        op("vector", lambda e: e.tensor_scalar(coefb[:, 1:2], coefb[:, 0:1],
                                               1e-6, 1.0, OP.add, OP.max),
           ev=f"vCB1_{gc}")
        wt("vector", f"vCB1_{gc}")
        op("vector", lambda e: e.reciprocal(coefb[:, 2:3], coefb[:, 1:2]),
           ev=f"vCB2_{gc}")
        wt("vector", f"vCB2_{gc}")
        op("vector", lambda e: e.tensor_scalar(coefb[:, 3:4], coefb[:, 2:3],
                                               -LR, None, OP.mult))
        op("vector", lambda e: e.tensor_scalar(coefb[:, 4:5], coefb[:, 2:3],
                                               -2.0 * LR, None, OP.mult),
           ev=f"vCB3_{gc}")
        wt("vector", f"vCB3_{gc}")
        op("vector", lambda e: e.scalar_tensor_tensor(
            aeff[:, :], accAT[:, :], coefb[0:R, 4:5], a2t[:, :], OP.mult, OP.add))
        op("vector", lambda e: e.scalar_tensor_tensor(
            beffT[:, :], accBT[:, :], coefb[:, 3:4], inBT_sb[:, 0:128],
            OP.mult, OP.add), ev=f"vEFF_{gc}")

    def B_base(gc):
        s, c = divmod(gc, NCH)
        q = gc % 2
        x3 = gc % 3
        xq, hfq = xbuf[x3], hfm[q]
        st2 = stat2[q]
        rep = reps[s]
        hfv = v3(hfq[:, :], CH)
        h2fv = v3(h2fm[:, :], CH)

        wt("tensor", f"T1_{gc}")
        wt("tensor", f"sVEV_{gc}")
        wt("tensor", f"vPE_{gc}")
        wt("tensor", f"vOUT_{gc - 1}")
        wt("tensor", f"sGEV_{gc - 1}")

        def base_pin(e):
            for ct in range(2):
                for kt in range(KT):
                    lh = hfv[:, kt, 128 * ct:128 * (ct + 1)]
                    e.matmul(pbig[:, 1024 * ct + 0:1024 * ct + 512], lh,
                             wbT_v[:, kt, 0:512],
                             start=(kt == 0), stop=False, skip_group_check=True)
                    e.matmul(pbig[:, 1024 * ct + 512:1024 * ct + 1024], lh,
                             wbT_v[:, kt, 512:1024],
                             start=(kt == 0), stop=False, skip_group_check=True)
                    e.matmul(psm[:, 32 + R * ct:32 + R * (ct + 1)], lh,
                             inBT_v[:, kt, :], start=(kt == 0),
                             stop=(kt == KT - 1), skip_group_check=True)
        op("tensor", base_pin)
        wt("tensor", f"vEFF_{gc}")

        def mid_group(e):
            last = None
            for kk in range(KT):
                last = e.matmul(psm[0:R, 64:64 + CH], beffT[:, R * kk:R * (kk + 1)],
                                hfv[:, kk, 0:CH], start=(kk == 0),
                                stop=(kk == KT - 1), skip_group_check=True)
            return last
        op("tensor", mid_group, ev=f"tMID_{gc}")

        wt("vector", f"tMID_{gc}")
        op("vector", lambda e: e.tensor_copy(midT_sb[:, :], psm[0:R, 64:64 + CH]))
        op("vector", lambda e: e.tensor_copy(pin_sb[:, :], psm[:, 32:64]),
           ev=f"vMID_{gc}")

        wt("tensor", f"vMID_{gc}")

        def lora_mms(e):
            last = None
            for ct in range(2):
                for hh in range(2):
                    last = e.matmul(pbig[:, 1024 * ct + 512 * hh:1024 * ct + 512 * (hh + 1)],
                                    midT_sb[0:R, 128 * ct:128 * (ct + 1)],
                                    aeff[0:R, 512 * hh:512 * (hh + 1)],
                                    start=False, stop=True, skip_group_check=True)
            return last
        op("tensor", lora_mms, ev=f"tBMM_{gc}")

        wt("vector", f"tBMM_{gc}")
        for ct in range(2):
            op("vector", lambda e, ct=ct: e.scalar_tensor_tensor(
                x2_tm[:, D * ct:D * (ct + 1)], pbig[:, 1024 * ct:1024 * (ct + 1)],
                1.0, rep["ga"][:, :], OP.mult, OP.mult))
        for ct in range(2):
            op("vector", lambda e, ct=ct: e.scalar_tensor_tensor(
                x2_tm[:, D * ct:D * (ct + 1)], x2_tm[:, D * ct:D * (ct + 1)],
                1.0, xq[:, D * ct:D * (ct + 1)], OP.mult, OP.add,
                accum_out=st2[:, ct:ct + 1]),
                ev=(f"vX2_{gc}" if ct == 1 else None))

        wt("scalar", f"vX2_{gc}")
        for ct in range(2):
            op("scalar", lambda e, ct=ct: e.activation(
                junk_s[:, :], x2_tm[:, D * ct:D * (ct + 1)], ACT.Square,
                scale=1.0 / 32.0, accum_out=st2[:, 8 + ct:9 + ct]))
        _ln_algebra(st2, 0, 4, 8, f"sS3_{gc}")

        wt("vector", f"sS3_{gc}")
        op("vector", lambda e: e.reciprocal(st2[:, 6:8], st2[:, 8:10]),
           ev=f"vR3_{gc}")
        wt("vector", f"vR3_{gc}")
        wt("vector", f"T3_{gc - 1}")
        for ct in range(2):
            op("vector", lambda e, ct=ct: e.scalar_tensor_tensor(
                h2_tm[:, D * ct:D * (ct + 1)], x2_tm[:, D * ct:D * (ct + 1)],
                st2[:, 4 + ct:5 + ct], rep["cb1"][:, :], OP.subtract, OP.mult))
        for ct in range(2):
            op("vector", lambda e, ct=ct: e.scalar_tensor_tensor(
                h2_tm[:, D * ct:D * (ct + 1)], h2_tm[:, D * ct:D * (ct + 1)],
                st2[:, 6 + ct:7 + ct], rep["sb_"][:, :], OP.mult, OP.add),
                ev=(f"vH2_{gc}" if ct == 1 else None))

        wt("sync", f"vH2_{gc}")
        wt("sync", f"tS_{gc - 1}")
        for ct in range(2):
            dma("sync", f"q3{gc % 2}", h2fv[:, :, 128 * ct:128 * (ct + 1)],
                h2_tm[:, D * ct:D * (ct + 1)], transpose=True,
                ev=(f"T3_{gc}" if ct == 1 else None))

    def B_tail(gc):
        s, c = divmod(gc, NCH)

        def da_mms(e):
            last = None
            for hh in range(2):
                for ct in range(2):
                    last = e.matmul(pda[0:R, 512 * hh:512 * (hh + 1)],
                                    pin_sb[:, R * ct:R * (ct + 1)],
                                    v_sb[:, 1024 * ct + 512 * hh:1024 * ct + 512 * (hh + 1)],
                                    start=(ct == 0), stop=(ct == 1),
                                    skip_group_check=True)
            return last
        op("tensor", da_mms, ev=f"tDMM_{gc}")

        wt("vector", f"tDMM_{gc}")
        wt("vector", f"tDB_{gc}")
        op("vector", lambda e: e.tensor_add(accAT[:, :], accAT[:, :], pda[0:R, 0:D]))
        op("vector", lambda e: e.tensor_add(accBT[:, :], accBT[:, :], pdb[:, :]),
           ev=f"vACC_{gc}")
        if c == NCH - 1 and s + 1 < NSEQ:
            op("vector", lambda e: e.memset(accAT[:, :], 0.0))
            op("vector", lambda e: e.memset(accBT[:, :], 0.0), ev=f"vRST{s + 1}")

    def gp_norm_squares(gc):
        s, c = divmod(gc, NCH)
        wt("scalar", f"vACC_{gc - 1}")
        if c == 0:
            wt("scalar", f"vRST{s}")
        op("scalar", lambda e: e.activation(junk_s[0:R, :], accAT[:, :], ACT.Square,
                                            accum_out=statn[0:R, 0:1]))
        op("scalar", lambda e: e.activation(junk_s[:, 0:128], accBT[:, :], ACT.Square,
                                            accum_out=statn[:, 1:2]), ev=f"sQF_{gc}")
        wt("scalar", f"sQF_{gc}")
        op("scalar", lambda e: e.copy(statnb[0:R, 0:1], statn[0:R, 0:1]))
        op("scalar", lambda e: e.copy(statnb[:, 1:2], statn[:, 1:2]),
           ev=f"gNRM_{gc}")

    def B_ffn(gc):
        s, c = divmod(gc, NCH)
        o2 = gc % 2
        h2fv = v3(h2fm[:, :], CH)

        wt("tensor", f"T3_{gc}")
        wt("tensor", f"vPE_{gc + 1}")
        if gc + 1 >= NSEQ * NCH:
            wt("tensor", f"vX2_{gc}")
        wt("tensor", "L_win")
        wt("tensor", "L_bdext")
        wt("tensor", "L_bd2")
        wt("tensor", "L_woutext")

        def s_mms(e):
            last = None
            for kt in range(KT):
                last = e.matmul(psm[0:M, 64:64 + CH], win_v[:, kt, :],
                                h2fv[:, kt, 0:CH], start=(kt == 0),
                                stop=(kt == KT - 1), skip_group_check=True)
            return last
        op("tensor", s_mms, ev=f"tS_{gc}")

        wt("scalar", f"tS_{gc}")
        wt("scalar", "L_binc")
        op("scalar", lambda e: e.activation(sext[0:M, :], psm[0:M, 64:64 + CH],
                                            ACT.Identity, bias=binc_sb[:, :]),
           ev=f"sSEV_{gc}")

        wt("tensor", f"sSEV_{gc}")
        wt("tensor", f"sVEV_{gc + 1}")

        def t_mms(e):
            last = None
            for g8 in range(8):
                last = e.matmul(pbig[:, 256 * g8:256 * (g8 + 1)],
                                bdext_sb[0:M + 1, 128 * g8:128 * (g8 + 1)],
                                sext[0:M + 1, :], start=True, stop=True,
                                skip_group_check=True)
            return last
        op("tensor", t_mms, ev=f"tT_{gc}")

        wt("scalar", f"tT_{gc}")
        op("scalar", lambda e: e.activation(t_sb[:, :], pbig[:, 0:2048], ACT.Relu),
           ev=f"sRLV_{gc}")

        wt("tensor", f"sRLV_{gc}")

        def g_mms(e):
            last = None
            for g8 in range(8):
                last = e.matmul(psm[0:M, 64:64 + CH], bd2_v[:, g8, :],
                                t_sb[:, 256 * g8:256 * (g8 + 1)],
                                start=(g8 == 0), stop=(g8 == 7),
                                skip_group_check=True)
            return last
        op("tensor", g_mms, ev=f"tG_{gc}")

        wt("scalar", f"tG_{gc}")
        wt("scalar", "L_b2c")
        op("scalar", lambda e: e.activation(gext[0:M, :], psm[0:M, 64:64 + CH],
                                            ACT.Identity, bias=b2c_sb[:, :]),
           ev=f"sGEV_{gc}")

        wt("tensor", f"sGEV_{gc}")
        wt("tensor", f"vWOF{s}")

        def out_mms(e):
            last = None
            for ct in range(2):
                for oh in range(2):
                    last = e.matmul(pbig[:, 1024 * ct + 512 * oh:1024 * ct + 512 * (oh + 1)],
                                    gext[0:M + 1, 128 * ct:128 * (ct + 1)],
                                    wouteff[0:M + 1, 512 * oh:512 * (oh + 1)],
                                    start=True, stop=True, skip_group_check=True)
            return last
        op("tensor", out_mms, ev=f"tFMM_{gc}")

        wt("vector", f"tFMM_{gc}")
        if c == NCH - 1 and s + 1 < NSEQ:
            op("vector", lambda e, s=s: e.tensor_tensor(
                wouteff[:, :], woutext_sb[:, :], reps[s + 1]["gb"][0:M + 1, :],
                OP.mult), ev=f"vWOF{s + 1}")
        wt("vector", f"Dst_{gc - 2}")
        ob = obuf[o2]
        for ct in range(2):
            op("vector", lambda e, ct=ct: e.scalar_tensor_tensor(
                ob[:, D * ct:D * (ct + 1)], pbig[:, 1024 * ct:1024 * (ct + 1)],
                1.0, x2_tm[:, D * ct:D * (ct + 1)], OP.mult, OP.add),
                ev=(f"vOUT_{gc}" if ct == 1 else None))

        wt("sync", f"vOUT_{gc}")
        rows = (s * NCH + c) * CH
        dma("sync", f"st{gc % 2}",
            dout[rows:rows + CH, :].rearrange("(a p) c -> p a c", p=128),
            v3(ob[:, :], D)[:, :, :], ev=f"Dst_{gc}")

    NG = NSEQ * NCH
    for g0 in range(min(2, NG)):
        A_load(g0)
        A_ln(g0)
        A_tr(g0)
    if NG > 2:
        A_load(2)
    gp_norm_squares(0)
    B_V(0)
    if NG > 2:
        A_ln(2)
    B_base(0)
    if NG > 2:
        A_tr(2)
    for j in range(NG):
        B_tail(j)
        if j + 3 < NG:
            A_load(j + 3)
        if j + 1 < NG:
            gp_norm_squares(j + 1)
            B_V(j + 1)
        if j + 3 < NG:
            A_ln(j + 3)
        B_ffn(j)
        if j + 1 < NG:
            B_base(j + 1)
        if j + 3 < NG:
            A_tr(j + 3)

    waitval("sync", "st0", cnt["st0"])
    waitval("sync", "st1", cnt["st1"])

    with nc.Block() as block:
        @block.tensor
        def _(e):
            for f in prog["tensor"]:
                f(e)

        @block.vector
        def _(e):
            for f in prog["vector"]:
                f(e)

        @block.scalar
        def _(e):
            for f in prog["scalar"]:
                f(e)

        @block.sync
        def _(e):
            for f in prog["sync"]:
                f(e)

        @block.gpsimd
        def _(e):
            for f in prog["gpsimd"]:
                f(e)

    return nc


def kernel(**inputs):
    ac = np.ascontiguousarray
    x = inputs["L0_attn_x"].astype(np.float32, copy=False)
    Wb = inputs["L0_attn_Wb"]
    Wp = inputs["L0_attn_Wp"]
    inA = inputs["L0_attn_init_A"]
    inB = inputs["L0_attn_init_B"]
    gamma = inputs["L0_attn_ln_gamma"]
    beta = inputs["L0_attn_ln_beta"]
    Win = inputs["L0_ffn_Win"]
    bin_ = inputs["L0_ffn_bin_"]
    W1 = inputs["L0_ffn_W1"]
    b1 = inputs["L0_ffn_b1"]
    W2 = inputs["L0_ffn_W2"]
    b2 = inputs["L0_ffn_b2"]
    Wout = inputs["L0_ffn_Wout"]
    bout = inputs["L0_ffn_bout"]
    cmod = inputs["L0_c"]
    wmod = inputs["L0_wmod"]

    bdext = np.zeros((M + 1, D), dtype=np.float32)
    bd2 = np.zeros((128, 8 * M), dtype=np.float32)
    for m in range(M):
        bdext[m, 16 * m:16 * m + 16] = W1[m]
        bdext[M, 16 * m:16 * m + 16] = b1[m]
        bd2[16 * (m % 8):16 * (m % 8) + 16, 64 * (m // 8) + m] = W2[m]
    shared = {
        "wbT": ac(np.asarray(Wb).T), "wpT": ac(np.asarray(Wp).T),
        "wp": ac(np.asarray(Wp)),
        "inA": ac(np.asarray(inA)), "inAT": ac(np.asarray(inA).T),
        "inBT": ac(np.asarray(inB).T),
        "gamma": ac(np.asarray(gamma).reshape(1, D)),
        "beta": ac(np.asarray(beta).reshape(1, D)),
        "win": ac(np.asarray(Win)),
        "binc": ac(np.asarray(bin_).reshape(M, 1)),
        "b2c": ac(np.asarray(b2).reshape(M, 1)),
        "bdext": bdext, "bd2": bd2,
        "woutext": ac(np.concatenate([np.asarray(Wout),
                                      np.asarray(bout).reshape(1, D)], axis=0)),
        "wmod": ac(np.asarray(wmod)),
    }
    nc = build_core()
    in_maps = []
    for i in range(NCORE):
        m = dict(shared)
        m["x"] = ac(np.asarray(x[2 * i:2 * i + 2]).reshape(NSEQ * NCH * CH, D))
        m["cvecT"] = ac(np.asarray(cmod[2 * i:2 * i + 2]).T)
        in_maps.append(m)
    res = run_bass_kernel_spmd(nc, in_maps, core_ids=list(range(NCORE)))
    out = np.concatenate(
        [r["out"].reshape(NSEQ, NCH * CH, D) for r in res.results], axis=0)
    return out


# version 20
# speedup vs baseline: 1.0432x; 1.0432x over previous
import numpy as np
import concourse.bass as bass
import concourse.mybir as mybir
from concourse.bass_utils import run_bass_kernel_spmd

F32 = mybir.dt.float32
BF16 = mybir.dt.bfloat16
OP = mybir.AluOpType
ACT = mybir.ActivationFunctionType

D = 1024
KT = 8
CH = 256
NCH = 16
NSEQ = 2
R = 16
M = 64
EPS = 1e-5
LR = 0.02
NCORE = 8


def build_core():
    nc = bass.Bass("TRN2", target_bir_lowering=False)

    dx = nc.dram_tensor("x", [NSEQ * NCH * CH, D], F32, kind="ExternalInput").ap()
    dout = nc.dram_tensor("out", [NSEQ * NCH * CH, D], F32, kind="ExternalOutput").ap()
    dwbT = nc.dram_tensor("wbT", [D, D], F32, kind="ExternalInput").ap()
    dwpT = nc.dram_tensor("wpT", [D, D], F32, kind="ExternalInput").ap()
    dwp = nc.dram_tensor("wp", [D, D], F32, kind="ExternalInput").ap()
    dinA = nc.dram_tensor("inA", [D, R], F32, kind="ExternalInput").ap()
    dinAT = nc.dram_tensor("inAT", [R, D], F32, kind="ExternalInput").ap()
    dinBT = nc.dram_tensor("inBT", [D, R], F32, kind="ExternalInput").ap()
    dgamma = nc.dram_tensor("gamma", [1, D], F32, kind="ExternalInput").ap()
    dbeta = nc.dram_tensor("beta", [1, D], F32, kind="ExternalInput").ap()
    dwin = nc.dram_tensor("win", [D, M], F32, kind="ExternalInput").ap()
    dbinc = nc.dram_tensor("binc", [M, 1], F32, kind="ExternalInput").ap()
    db2c = nc.dram_tensor("b2c", [M, 1], F32, kind="ExternalInput").ap()
    dbdext = nc.dram_tensor("bdext", [M + 1, D], F32, kind="ExternalInput").ap()
    dbd2 = nc.dram_tensor("bd2", [128, 8 * M], F32, kind="ExternalInput").ap()
    dwoutext = nc.dram_tensor("woutext", [M + 1, D], F32, kind="ExternalInput").ap()
    dcvecT = nc.dram_tensor("cvecT", [D, NSEQ], F32, kind="ExternalInput").ap()
    dwmod = nc.dram_tensor("wmod", [D, 6 * D], F32, kind="ExternalInput").ap()
    dmodp = nc.dram_tensor("modp", [NSEQ, 6 * D], F32).ap()

    def sb(name, shape, dt=BF16, tr=False):
        return nc.alloc_sbuf_tensor(name, shape, dt, is_transpose_dest=tr).ap()

    wbT_sb = sb("wbT_sb", [128, KT * D])
    wpT_sb = sb("wpT_sb", [128, KT * D])
    win_sb = sb("win_sb", [128, KT * M])
    inBT_sb = sb("inBT_sb", [128, KT * R])
    inA_sb = sb("inA_sb", [128, KT * R])
    wpa_sb = sb("wpa_sb", [128, KT * R], tr=True)
    bdext_sb = sb("bdext_sb", [M + 1, D])
    bd2_sb = sb("bd2_sb", [128, 8 * M])
    woutext_sb = sb("woutext_sb", [M + 1, D])
    a2t = sb("a2t", [R, D])
    binc_sb = sb("binc_sb", [M, 1], F32)
    b2c_sb = sb("b2c_sb", [M, 1], F32)
    gcolf = sb("gcolf", [128, KT], F32)
    bcolb = sb("bcolb", [128, KT])
    biasV = sb("biasV", [1, D])
    negbV = sb("negbV", [1, D])
    biasP = sb("biasP", [1, R])
    negbP = sb("negbP", [1, R])
    ones1 = sb("ones1", [1, 128])
    oneh = sb("oneh", [1, 128])
    ones128 = sb("ones128", [128, 128])
    F32R = mybir.dt.float32r
    wmR = sb("wmR", [128, KT * 512], F32R)
    silur = sb("silur", [128, KT * NSEQ], F32R)
    silucT = sb("silucT", [128, KT * NSEQ], F32)

    reps = [{g: sb(f"rep_{g}_{s}", [128, D]) for g in
             ("sa", "ca1", "ga", "sb_", "cb1", "gb")} for s in range(NSEQ)]

    xbuf = [sb(f"x_tm{i}", [128, 2 * D], F32) for i in range(3)]
    hbuf = [sb(f"h_tm{i}", [128, 2 * D]) for i in range(2)]
    d2buf = [sb(f"d2_tm{i}", [128, 2 * D]) for i in range(2)]
    hfm = [sb(f"hfm{i}", [128, 2 * D], tr=True) for i in range(2)]
    xnfmc = [sb(f"xnfmc{i}", [128, 2 * D], tr=True) for i in range(3)]
    xnfm = [sb(f"xnfm{i}", [128, KT * 129]) for i in range(2)]
    h2_tm = sb("h2_tm", [128, 2 * D])
    h2fm = sb("h2fm", [128, 2 * D], tr=True)
    v_sb = sb("v_sb", [128, 2 * D])
    pin_sb = sb("pin_sb", [128, 2 * R])
    perr_sb = sb("perr_sb", [128, 2 * R])
    midT_sb = sb("midT_sb", [R, CH])
    t_sb = sb("t_sb", [128, 2 * D])
    sext = sb("sext", [M + 1, CH])
    gext = sb("gext", [M + 1, CH])
    x2_tm = sb("x2_tm", [128, 2 * D], F32)
    wouteff = sb("wouteff", [M + 1, D])
    obuf = [sb(f"out_tm{i}", [128, 2 * D], F32) for i in range(2)]
    wpaT_sb = t_sb[0:R, 0:D]
    cvec_sb = obuf[1][0:NSEQ, 0:D]
    modtmp = obuf[1][0:NSEQ, D:2 * D]
    modt2 = obuf[0][0:NSEQ, 0:D]
    modt3 = obuf[0][0:NSEQ, D:2 * D]
    junk_v = sb("junk_v", [128, D])
    junk_g = sb("junk_g", [128, 16])
    junk_s = sb("junk_s", [128, D])
    cst = sb("cst", [128, 4], F32)
    stat = [sb(f"stat{i}", [128, 24], F32) for i in range(2)]
    stat2 = [sb(f"stat2_{i}", [128, 16], F32) for i in range(2)]
    statn = sb("statn", [128, 2], F32)
    statnb = sb("statnb", [128, 2])
    coefb = sb("coefb", [128, 6], F32)
    accAT = sb("accAT", [R, D], F32)
    accBT = sb("accBT", [128, 128], F32)
    aeff = sb("aeff", [R, D])
    beffT = sb("beffT", [128, 128])

    pbig = nc.alloc_psum_tensor("pbig", [128, 2048], F32).ap()
    psm = nc.alloc_psum_tensor("psm", [128, 512], F32).ap()
    pda = nc.alloc_psum_tensor("pda", [128, 1024], F32).ap()
    pdb = nc.alloc_psum_tensor("pdb", [128, 128], F32).ap()

    def v3(ap, inner):
        return ap.rearrange("p (a b) -> p a b", b=inner)

    wbT_v = v3(wbT_sb[:, :], D)
    wpT_v = v3(wpT_sb[:, :], D)
    win_v = v3(win_sb[:, :], M)
    inBT_v = v3(inBT_sb[:, :], R)
    inA_v = v3(inA_sb[:, :], R)
    wpa_v = v3(wpa_sb[:, :], R)
    wmR_v = v3(wmR[:, :], 512)
    silur_v = v3(silur[:, :], NSEQ)
    bd2_v = v3(bd2_sb[:, :], M)

    sem = {k: nc.alloc_semaphore(k) for k in ("t", "v", "s", "g", "dG", "dY", "dQ", "x0", "x1", "x2", "q10", "q11", "q20", "q21", "q22", "q30", "q31", "st0", "st1")}
    cnt = {k: 0 for k in sem}
    EV = {}
    prog = {k: [] for k in ("tensor", "vector", "scalar", "sync", "gpsimd")}
    ENGKEY = {"tensor": "t", "vector": "v", "scalar": "s",
              "sync": "dY", "gpsimd": "g"}

    def op(eng, fn, ev=None):
        key = ENGKEY[eng]
        val = None
        if ev is not None:
            cnt[key] += 1
            val = cnt[key]
            EV[ev] = (key, val)
        h = sem[key]

        def run(e, fn=fn, val=val):
            inst = fn(e)
            if val is not None:
                inst.then_inc(h, 1)
        prog[eng].append(run)

    def wt(eng, ev):
        if ev not in EV:
            return
        key, val = EV[ev]
        h = sem[key]
        prog[eng].append(lambda e, h=h, val=val: e.wait_ge(h, val))

    def dma(eng, ring, out, in_, ev=None, transpose=False, slow=False):
        cnt[ring] += 16
        val = cnt[ring]
        if ev is not None:
            EV[ev] = (ring, val)
        h = sem[ring]

        def run(e, out=out, in_=in_, transpose=transpose, slow=slow):
            e.dma_start(out, in_, transpose=transpose,
                        allow_slow_non_contiguous=slow).then_inc(h, 16)
        prog[eng].append(run)

    def waitval(eng, ring, val):
        h = sem[ring]
        prog[eng].append(lambda e, h=h, val=val: e.wait_ge(h, val))

    def tiled_load(name_ev, dst_v, src, cols, nt=KT, slow=False):
        dma("gpsimd", "dG", dst_v,
            src.rearrange("(t p) c -> p t c", p=128), ev=name_ev, slow=slow)

    wp_stage = [v3(xnfmc[0][:, :], 512), v3(xnfmc[1][:, :], 512)]

    def wp_half_load(half, ev):
        for grp in range(2):
            dma("gpsimd", "dG", wp_stage[grp][:, 0:4, :],
                dwp[512 * grp:512 * (grp + 1),
                    512 * half:512 * (half + 1)].rearrange("(t p) c -> p t c", p=128),
                ev=(ev if grp == 1 else None))
    tiled_load("L_wpT", wpT_v[:, :, :], dwpT[:, :], D)
    tiled_load("L_wbT", wbT_v[:, :, :], dwbT[:, :], D)
    tiled_load("L_win", win_v[:, :, :], dwin[:, :], M)
    tiled_load("L_inBT", inBT_v[:, :, :], dinBT[:, :], R)
    tiled_load("L_inA", inA_v[:, :, :], dinA[:, :], R)
    dma("gpsimd", "dG", bdext_sb[:, :], dbdext[:, :], ev="L_bdext")
    dma("gpsimd", "dG", bd2_sb[:, :], dbd2[:, :], ev="L_bd2")
    dma("gpsimd", "dG", woutext_sb[:, :], dwoutext[:, :], ev="L_woutext")
    dma("gpsimd", "dG", a2t[:, :], dinAT[:, :], ev="L_inAT")
    dma("gpsimd", "dG", binc_sb[:, :], dbinc[:, :], ev="L_binc")
    dma("gpsimd", "dG", b2c_sb[:, :], db2c[:, :], ev="L_b2c")
    dma("gpsimd", "dG", gcolf[:, :],
        bass.AP(dgamma.tensor, 0, [[1, 128], [128, KT]]), ev="L_gcol", slow=True)
    dma("gpsimd", "dG", bcolb[:, :],
        bass.AP(dbeta.tensor, 0, [[1, 128], [128, KT]]), ev="L_bcol", slow=True)
    tiled_load("L_cvt", v3(silucT[:, :], NSEQ)[:, :, :], dcvecT[:, :], NSEQ, slow=True)
    waitval("gpsimd", "dG", cnt["dG"])
    op("gpsimd", lambda e: e.memset(junk_g[0:1, 0:1], 0.0), ev="gLD")


    CONSTS = (EPS, LR * LR, 1.0 / D, -1.0)
    for j, val in enumerate(CONSTS):
        op("vector", lambda e, j=j, val=val: e.memset(cst[:, j:j + 1], val))
        nc.const_aps.register(F32, val, cst[:, j:j + 1])
    op("vector", lambda e: e.memset(sext[M:M + 1, :], 1.0))
    op("vector", lambda e: e.memset(gext[M:M + 1, :], 1.0))
    op("vector", lambda e: e.memset(ones1[:, :], 1.0))
    op("vector", lambda e: e.memset(oneh[:, :], 0.0), ev="v_oh0")
    wt("vector", "v_oh0")
    op("vector", lambda e: e.memset(oneh[0:1, 127:128], 1.0))
    op("vector", lambda e: e.memset(ones128[:, :], 1.0))
    op("vector", lambda e: e.memset(accAT[:, :], 0.0))
    op("vector", lambda e: e.memset(accBT[:, :], 0.0), ev="vRST0")
    wt("vector", "gLD")
    op("vector", lambda e: e.tensor_scalar(a2t[:, :], a2t[:, :], 2.0, None, OP.mult),
       ev="v_a2t")

    wt("tensor", "gLD")
    for half in range(2):
        if half == 1:
            wt("gpsimd", "t_wpah0")
        wp_half_load(half, f"L_wph{half}")
        wt("tensor", f"L_wph{half}")

        def wpa_mms(e, half=half):
            last = None
            for ht in range(KT):
                last = e.matmul(pda[0:R, 512 * half:512 * (half + 1)],
                                inA_v[:, ht, :], wp_stage[ht // 4][:, ht % 4, :],
                                start=(ht == 0), stop=(ht == KT - 1),
                                skip_group_check=True)
            return last
        op("tensor", wpa_mms, ev=("t_wpah0" if half == 0 else "t_wpa"))

    wt("vector", "t_wpa")
    op("vector", lambda e: e.tensor_copy(wpaT_sb[:, :], pda[0:R, 0:D]), ev="v_wpaT")

    wt("scalar", "v_wpaT")
    dma("scalar", "dQ", v3(wpa_sb[:, :], R)[:, :, :], wpaT_sb[:, :],
        ev="T_wpa", transpose=True)

    wt("tensor", "gLD")
    wt("tensor", "gLD")
    wt("tensor", "t_wpa")
    wt("tensor", "v_wpaT")

    def bias_mms(e):
        last = None
        for half in range(2):
            for kt in range(KT):
                last = e.matmul(pda[0:1, 512 * half + 0:512 * (half + 1)],
                                bcolb[:, kt:kt + 1],
                                wpT_v[:, kt, 512 * half:512 * (half + 1)],
                                start=(kt == 0), stop=(kt == KT - 1),
                                skip_group_check=True)
        return last
    op("tensor", bias_mms, ev="t_biasV")
    wt("tensor", "T_wpa")

    def biasp_mms(e):
        last = None
        for kt in range(KT):
            last = e.matmul(psm[0:1, 0:R], bcolb[:, kt:kt + 1], wpa_v[:, kt, :],
                            start=(kt == 0), stop=(kt == KT - 1),
                            skip_group_check=True)
        return last
    op("tensor", biasp_mms, ev="t_biasP")

    wt("vector", "t_biasV")
    op("vector", lambda e: e.tensor_copy(biasV[:, :], pda[0:1, 0:D]))
    op("vector", lambda e: e.tensor_scalar(negbV[:, :], biasV[:, :], -1.0, None, OP.mult))
    wt("vector", "t_biasP")
    op("vector", lambda e: e.tensor_copy(biasP[:, :], psm[0:1, 0:R]), ev="v_bp0")
    wt("vector", "v_bp0")
    op("vector", lambda e: e.tensor_scalar(negbP[:, :], biasP[:, :], -1.0, None, OP.mult),
       ev="v_bias")

    wt("vector", "gLD")
    wt("vector", "t_biasV")
    for kt in range(KT):
        op("vector", lambda e, kt=kt: e.tensor_scalar(
            wpT_v[:, kt, :], wpT_v[:, kt, :], gcolf[:, kt:kt + 1], None, OP.mult))
    wt("vector", "t_biasP")
    for kt in range(KT):
        op("vector", lambda e, kt=kt: e.tensor_scalar(
            wpa_v[:, kt, :], wpa_v[:, kt, :], gcolf[:, kt:kt + 1], None, OP.mult),
            ev=("v_wfold" if kt == KT - 1 else None))

    wt("scalar", "gLD")
    op("scalar", lambda e: e.activation(silur[:, :], silucT[:, :], ACT.Silu),
       ev="s_silu")

    for g in range(6):
        for hf in range(2):
            if not (g == 0 and hf == 0):
                wt("gpsimd", f"t_mod{g}_0" if hf == 1 else f"t_mod{g - 1}")
            dma("gpsimd", "dG", wmR_v[:, :, :],
                dwmod[:, 1024 * g + 512 * hf:1024 * g + 512 * (hf + 1)]
                .rearrange("(t p) c -> p t c", p=128),
                ev=f"L_wm{g}_{hf}")
            wt("tensor", f"L_wm{g}_{hf}")
            if g == 0 and hf == 0:
                wt("tensor", "s_silu")
                wt("tensor", "t_biasP")
                wt("tensor", "v_bias")

            def mod_mms(e, hf=hf):
                last = None
                for kt in range(KT):
                    last = e.matmul(pda[0:NSEQ, 512 * hf:512 * (hf + 1)],
                                    silur_v[:, kt, :], wmR_v[:, kt, :],
                                    start=(kt == 0), stop=(kt == KT - 1),
                                    skip_group_check=True)
                return last
            op("tensor", mod_mms, ev=(f"t_mod{g}" if hf == 1 else f"t_mod{g}_0"))

        wt("vector", f"t_mod{g}")
        if g > 0:
            wt("vector", f"D_modst{g - 1}")
        op("vector", lambda e: e.tensor_copy(modtmp[:, :], pda[0:NSEQ, 0:D]),
           ev=f"v_modev{g}")
        if g in (1, 4):
            op("vector", lambda e: e.tensor_scalar(
                modtmp[:, :], modtmp[:, :], 1.0, None, OP.add), ev=f"v_modp{g}")
        elif g in (2, 5):
            wt("scalar", f"v_modev{g}")
            op("scalar", lambda e: e.sign(modt2[:, :], modtmp[:, :]))
            op("scalar", lambda e: e.activation(modt3[:, :], modtmp[:, :], ACT.Abs),
               ev=f"s_modg{g}")
            wt("vector", f"s_modg{g}")
            op("vector", lambda e: e.tensor_tensor(
                junk_v[0:NSEQ, 0:D], modt2[:, :], modt2[:, :], OP.mult))
            op("vector", lambda e: e.tensor_scalar(
                junk_v[0:NSEQ, 0:D], junk_v[0:NSEQ, 0:D], -1.0, 1.0, OP.mult, OP.add))
            op("vector", lambda e: e.tensor_add(
                modt2[:, :], modt2[:, :], junk_v[0:NSEQ, 0:D]))
            op("vector", lambda e: e.tensor_single_scalar(
                modt3[:, :], modt3[:, :], 0.25, OP.max))
            op("vector", lambda e: e.tensor_mul(
                modtmp[:, :], modt2[:, :], modt3[:, :]), ev=f"v_modp{g}")
        else:
            op("vector", lambda e: e.engine_nop(), ev=f"v_modp{g}")
        wt("gpsimd", f"v_modp{g}")
        dma("gpsimd", "dG", dmodp[:, 1024 * g:1024 * (g + 1)], modtmp[:, :],
            ev=f"D_modst{g}")

    waitval("gpsimd", "dG", cnt["dG"])
    GORDER = ("sa", "ca1", "ga", "sb_", "cb1", "gb")
    for s in range(NSEQ):
        for g in range(6):
            wt("gpsimd", f"D_modst{g}")
            dma("gpsimd", "dG", reps[s][GORDER[g]][:, :],
                bass.AP(dmodp.tensor, s * 6 * D + 1024 * g, [[0, 128], [1, D]]),
                ev=f"D_bc{s}_{g}")

    wt("vector", f"D_bc{NSEQ - 1}_5")
    wt("vector", "L_woutext")
    op("vector", lambda e: e.tensor_tensor(
        wouteff[:, :], woutext_sb[:, :], reps[0]["gb"][0:M + 1, :], OP.mult),
        ev="vWOF0")

    for eng in ("tensor", "vector", "scalar", "sync"):
        for ev in ("t_mod5", "v_modp5", "s_modg5", f"D_bc{NSEQ - 1}_5", "v_wfold",
                   "v_a2t", "v_bias"):
            wt(eng, ev)

    def A_load(gc):
        s, c = divmod(gc, NCH)
        x3 = gc % 3
        xq = xbuf[x3]
        wt("gpsimd", f"vX2_{gc - 3}")
        rows = (s * NCH + c) * CH
        dma("gpsimd", f"x{gc % 3}",
            v3(xq[:, :], D)[:, :, :],
            dx[rows:rows + CH, :].rearrange("(a p) c -> p a c", p=128),
            ev=f"Dx_{gc}")

    def A_ln(gc):
        s, c = divmod(gc, NCH)
        q = gc % 2
        x3 = gc % 3
        xq, hq, d2q, hfq, xnq = xbuf[x3], hbuf[q], d2buf[q], hfm[q], xnfm[q]
        xncq = xnfmc[q]
        st = stat[q]
        rep = reps[s]

        waitval("vector", f"x{gc % 3}", EV[f"Dx_{gc}"][1])
        for ct in range(2):
            op("vector", lambda e, ct=ct: e.tensor_scalar(
                junk_v[:, :], xq[:, D * ct:D * (ct + 1)], 1.0 / D, None,
                OP.mult, OP.add, accum_out=st[:, 4 + ct:5 + ct]),
                ev=(f"vS1_{gc}" if ct == 1 else None))

        waitval("scalar", f"x{gc % 3}", EV[f"Dx_{gc}"][1])
        wt("scalar", f"vD2_{gc - 2}")
        for ct in range(2):
            op("scalar", lambda e, ct=ct: e.activation(
                junk_s[:, :], xq[:, D * ct:D * (ct + 1)], ACT.Square,
                scale=1.0 / 32.0, accum_out=st[:, 16 + ct:17 + ct]))
        wt("scalar", f"vS1_{gc}")
        _ln_algebra(st, None, 4, 16, f"sS1_{gc}")

        wt("vector", f"sS1_{gc}")
        op("vector", lambda e: e.reciprocal(st[:, 6:8], st[:, 16:18]),
           ev=f"vR1_{gc}")
        wt("vector", f"vR1_{gc}")
        wt("vector", f"tDB_{gc - 2}")
        wt("vector", f"T1_{gc - 2}")
        for ct in range(2):
            op("vector", lambda e, ct=ct: e.scalar_tensor_tensor(
                hq[:, D * ct:D * (ct + 1)], xq[:, D * ct:D * (ct + 1)],
                st[:, 4 + ct:5 + ct], rep["ca1"][:, :], OP.subtract, OP.mult))
        for ct in range(2):
            op("vector", lambda e, ct=ct: e.scalar_tensor_tensor(
                hq[:, D * ct:D * (ct + 1)], hq[:, D * ct:D * (ct + 1)],
                st[:, 6 + ct:7 + ct], rep["sa"][:, :], OP.mult, OP.add,
                accum_out=st[:, 8 + ct:9 + ct]),
                ev=(f"vH_{gc}" if ct == 1 else None))

        wt("scalar", f"vH_{gc}")
        for ct in range(2):
            op("scalar", lambda e, ct=ct: e.activation(
                junk_s[:, :], hq[:, D * ct:D * (ct + 1)], ACT.Square,
                scale=1.0 / 32.0, accum_out=st[:, 18 + ct:19 + ct]))
        _ln_algebra(st, 8, 12, 18, f"sS2_{gc}")

    def A_tr(gc):
        s, c = divmod(gc, NCH)
        q = gc % 2
        x3 = gc % 3
        xq, hq, d2q, hfq, xnq = xbuf[x3], hbuf[q], d2buf[q], hfm[q], xnfm[q]
        xncq = xnfmc[gc % 3]
        st = stat[q]

        wt("vector", f"sS2_{gc}")
        op("vector", lambda e: e.reciprocal(st[:, 14:16], st[:, 18:20]),
           ev=f"vR2_{gc}")
        wt("vector", f"vR2_{gc}")
        wt("vector", f"T2_{gc - 2}")
        for ct in range(2):
            op("vector", lambda e, ct=ct: e.tensor_scalar(
                d2q[:, D * ct:D * (ct + 1)], hq[:, D * ct:D * (ct + 1)],
                st[:, 12 + ct:13 + ct], st[:, 14 + ct:15 + ct],
                OP.subtract, OP.mult),
                ev=(f"vD2_{gc}" if ct == 1 else None))

        wt("sync", f"vD2_{gc}")
        wt("sync", f"vXN_{gc - 3}")
        wt("sync", f"vBND_{gc - 4}")
        wt("sync", f"tVc0_{gc - 3}")
        for ct in range(2):
            dma("sync", f"q2{gc % 3}",
                v3(xncq[:, :], CH)[:, :, 128 * ct:128 * (ct + 1)],
                d2q[:, D * ct:D * (ct + 1)], transpose=True,
                ev=(f"T2_{gc}" if ct == 1 else None))
        wt("sync", f"vH_{gc}")
        wt("sync", f"tBMM_{gc - 2}")
        for ct in range(2):
            dma("sync", f"q1{gc % 2}",
                v3(hfq[:, :], CH)[:, :, 128 * ct:128 * (ct + 1)],
                hq[:, D * ct:D * (ct + 1)], transpose=True,
                ev=(f"T1_{gc}" if ct == 1 else None))

        waitval("vector", f"q2{gc % 3}", EV[f"T2_{gc}"][1])
        op("vector", lambda e: e.tensor_copy(
            v3(xnq[:, :], 129)[:, :, 0:128], v3(xncq[:, :], CH)[:, :, 128:256]),
            ev=f"vXN_{gc}")
        if c > 0:
            prev = xnfm[(gc - 1) % 2]
            op("vector", lambda e: e.tensor_copy(
                v3(prev[:, :], 129)[:, :, 128:129],
                v3(xncq[:, :], CH)[:, :, 0:1]), ev=f"vBND_{gc - 1}")
        if c == NCH - 1:
            op("vector", lambda e: e.memset(
                v3(xnq[:, :], 129)[:, :, 128:129], 0.0), ev=f"vBND_{gc}")

    _fence_n = [0]

    def _ln_algebra(st, sumc, muc, e2c, ev):
        _fence_n[0] += 1
        f = f"sF{_fence_n[0]}"
        tm_ = e2c + 4
        if sumc is not None:
            op("scalar", lambda e: e.mul(st[:, muc:muc + 2],
                                         st[:, sumc:sumc + 2], 1.0 / D), ev=f + "a")
            wt("scalar", f + "a")
        op("scalar", lambda e: e.activation(st[:, tm_:tm_ + 2],
                                            st[:, muc:muc + 2], ACT.Square), ev=f + "b")
        wt("scalar", f + "b")
        op("scalar", lambda e: e.mul(st[:, tm_:tm_ + 2],
                                     st[:, tm_:tm_ + 2], -1.0), ev=f + "c")
        wt("scalar", f + "c")
        for j in range(2):
            op("scalar", lambda e, j=j: e.add(st[:, e2c + j:e2c + j + 1],
                                              st[:, e2c + j:e2c + j + 1],
                                              st[:, tm_ + j:tm_ + j + 1]),
               ev=(f + "d" if j == 1 else None))
        wt("scalar", f + "d")
        op("scalar", lambda e: e.activation(st[:, e2c:e2c + 2], st[:, e2c:e2c + 2],
                                            ACT.Sqrt, bias=EPS), ev=ev)

    def B_V(gc):
        s, c = divmod(gc, NCH)
        q = gc % 2
        xnv = v3(xnfm[q][:, :], 129)
        last_chunk = (c == NCH - 1)

        wt("tensor", f"vX2_{gc - 1}")
        wt("tensor", f"sGEV_{gc - 2}")
        wt("tensor", "v_wfold")
        wt("tensor", "v_a2t")

        xncv = v3(xnfmc[gc % 3][:, :], CH)

        def v_group_ct(e, ct, lh_of):
            last = None
            for kt in range(KT):
                lh = lh_of(kt)
                e.matmul(pbig[:, 1024 * ct + 0:1024 * ct + 512], lh,
                         wpT_v[:, kt, 0:512],
                         start=(kt == 0), stop=False, skip_group_check=True)
                e.matmul(pbig[:, 1024 * ct + 512:1024 * ct + 1024], lh,
                         wpT_v[:, kt, 512:1024],
                         start=(kt == 0), stop=False, skip_group_check=True)
                e.matmul(psm[:, R * ct:R * (ct + 1)], lh, wpa_v[:, kt, :],
                         start=(kt == 0), stop=False, skip_group_check=True)
            patch = last_chunk and ct == 1
            e.matmul(pbig[:, 1024 * ct + 0:1024 * ct + 512], ones1[:, :],
                     biasV[0:1, 0:512], start=False, stop=not patch,
                     skip_group_check=True)
            e.matmul(pbig[:, 1024 * ct + 512:1024 * ct + 1024], ones1[:, :],
                     biasV[0:1, 512:1024], start=False, stop=not patch,
                     skip_group_check=True)
            last = e.matmul(psm[:, R * ct:R * (ct + 1)], ones1[:, :],
                            biasP[0:1, :], start=False, stop=not patch,
                            skip_group_check=True)
            if patch:
                e.matmul(pbig[:, 1024 * ct + 0:1024 * ct + 512], oneh[:, :],
                         negbV[0:1, 0:512], start=False, stop=True,
                         skip_group_check=True)
                e.matmul(pbig[:, 1024 * ct + 512:1024 * ct + 1024], oneh[:, :],
                         negbV[0:1, 512:1024], start=False, stop=True,
                         skip_group_check=True)
                last = e.matmul(psm[:, R * ct:R * (ct + 1)], oneh[:, :],
                                negbP[0:1, :], start=False, stop=True,
                                skip_group_check=True)
            return last
        waitval("tensor", f"q2{gc % 3}", EV[f"T2_{gc}"][1])
        op("tensor", lambda e: v_group_ct(e, 0, lambda kt: xncv[:, kt, 1:129]),
           ev=f"tVc0_{gc}")
        wt("tensor", f"vXN_{gc}")
        wt("tensor", f"vBND_{gc}")
        op("tensor", lambda e: v_group_ct(e, 1, lambda kt: xnv[:, kt, 1:129]),
           ev=f"tVMM_{gc}")

        wt("vector", f"tVMM_{gc}")
        op("vector", lambda e: e.tensor_copy(perr_sb[:, :], psm[:, 0:2 * R]),
           ev=f"vPE_{gc}")

        wt("tensor", f"gNRM_{gc}")
        wt("tensor", f"vACC_{gc - 1}")

        def norm_mms(e):
            e.matmul(pda[:, 0:1], ones128[0:R, :], statnb[0:R, 0:1],
                     start=True, stop=False, skip_group_check=True)
            return e.matmul(pda[:, 0:1], ones128[:, :], statnb[:, 1:2],
                            start=False, stop=True, skip_group_check=True)
        op("tensor", norm_mms, ev=f"tNRM_{gc}")

        wt("tensor", f"vPE_{gc}")
        hq_ = hbuf[q]

        def db_mms(e):
            last = None
            for kk in range(KT):
                for ct in range(2):
                    last = e.matmul(pdb[:, R * kk:R * (kk + 1)],
                                    hq_[:, 1024 * ct + 128 * kk:1024 * ct + 128 * (kk + 1)],
                                    perr_sb[:, R * ct:R * (ct + 1)],
                                    start=(ct == 0), stop=(ct == 1),
                                    skip_group_check=True)
            return last
        op("tensor", db_mms, ev=f"tDB_{gc}")

        wt("scalar", f"tVMM_{gc}")
        op("scalar", lambda e: e.copy(v_sb[:, 0:D], pbig[:, 0:D]))
        op("scalar", lambda e: e.copy(v_sb[:, D:2 * D], pbig[:, D:2 * D]),
           ev=f"sVEV_{gc}")
        wt("scalar", f"tNRM_{gc}")
        op("scalar", lambda e: e.activation(coefb[:, 0:1], pda[:, 0:1], ACT.Sqrt,
                                            scale=LR * LR), ev=f"sSD_{gc}")

        wt("vector", f"sSD_{gc}")
        op("vector", lambda e: e.tensor_scalar(coefb[:, 1:2], coefb[:, 0:1],
                                               1e-6, 1.0, OP.add, OP.max),
           ev=f"vCB1_{gc}")
        wt("vector", f"vCB1_{gc}")
        op("vector", lambda e: e.reciprocal(coefb[:, 2:3], coefb[:, 1:2]),
           ev=f"vCB2_{gc}")
        wt("vector", f"vCB2_{gc}")
        op("vector", lambda e: e.tensor_scalar(coefb[:, 3:4], coefb[:, 2:3],
                                               -LR, None, OP.mult))
        op("vector", lambda e: e.tensor_scalar(coefb[:, 4:5], coefb[:, 2:3],
                                               -2.0 * LR, None, OP.mult),
           ev=f"vCB3_{gc}")
        wt("vector", f"vCB3_{gc}")
        op("vector", lambda e: e.scalar_tensor_tensor(
            aeff[:, :], accAT[:, :], coefb[0:R, 4:5], a2t[:, :], OP.mult, OP.add))
        op("vector", lambda e: e.scalar_tensor_tensor(
            beffT[:, :], accBT[:, :], coefb[:, 3:4], inBT_sb[:, 0:128],
            OP.mult, OP.add), ev=f"vEFF_{gc}")

    def B_base(gc):
        s, c = divmod(gc, NCH)
        q = gc % 2
        x3 = gc % 3
        xq, hfq = xbuf[x3], hfm[q]
        st2 = stat2[q]
        rep = reps[s]
        hfv = v3(hfq[:, :], CH)
        h2fv = v3(h2fm[:, :], CH)

        wt("tensor", f"T1_{gc}")
        wt("tensor", f"sVEV_{gc}")
        wt("tensor", f"vPE_{gc}")
        wt("tensor", f"vOUT_{gc - 1}")
        wt("tensor", f"sGEV_{gc - 1}")

        def base_pin(e):
            for ct in range(2):
                for kt in range(KT):
                    lh = hfv[:, kt, 128 * ct:128 * (ct + 1)]
                    e.matmul(pbig[:, 1024 * ct + 0:1024 * ct + 512], lh,
                             wbT_v[:, kt, 0:512],
                             start=(kt == 0), stop=False, skip_group_check=True)
                    e.matmul(pbig[:, 1024 * ct + 512:1024 * ct + 1024], lh,
                             wbT_v[:, kt, 512:1024],
                             start=(kt == 0), stop=False, skip_group_check=True)
                    e.matmul(psm[:, 32 + R * ct:32 + R * (ct + 1)], lh,
                             inBT_v[:, kt, :], start=(kt == 0),
                             stop=(kt == KT - 1), skip_group_check=True)
        op("tensor", base_pin)
        wt("tensor", f"vEFF_{gc}")

        def mid_group(e):
            last = None
            for kk in range(KT):
                last = e.matmul(psm[0:R, 64:64 + CH], beffT[:, R * kk:R * (kk + 1)],
                                hfv[:, kk, 0:CH], start=(kk == 0),
                                stop=(kk == KT - 1), skip_group_check=True)
            return last
        op("tensor", mid_group, ev=f"tMID_{gc}")

        wt("vector", f"tMID_{gc}")
        op("vector", lambda e: e.tensor_copy(midT_sb[:, :], psm[0:R, 64:64 + CH]))
        op("vector", lambda e: e.tensor_copy(pin_sb[:, :], psm[:, 32:64]),
           ev=f"vMID_{gc}")

        wt("tensor", f"vMID_{gc}")

        def lora_mms(e):
            last = None
            for ct in range(2):
                for hh in range(2):
                    last = e.matmul(pbig[:, 1024 * ct + 512 * hh:1024 * ct + 512 * (hh + 1)],
                                    midT_sb[0:R, 128 * ct:128 * (ct + 1)],
                                    aeff[0:R, 512 * hh:512 * (hh + 1)],
                                    start=False, stop=True, skip_group_check=True)
            return last
        op("tensor", lora_mms, ev=f"tBMM_{gc}")

        wt("vector", f"tBMM_{gc}")
        for ct in range(2):
            op("vector", lambda e, ct=ct: e.scalar_tensor_tensor(
                x2_tm[:, D * ct:D * (ct + 1)], pbig[:, 1024 * ct:1024 * (ct + 1)],
                1.0, rep["ga"][:, :], OP.mult, OP.mult))
        for ct in range(2):
            op("vector", lambda e, ct=ct: e.scalar_tensor_tensor(
                x2_tm[:, D * ct:D * (ct + 1)], x2_tm[:, D * ct:D * (ct + 1)],
                1.0, xq[:, D * ct:D * (ct + 1)], OP.mult, OP.add,
                accum_out=st2[:, ct:ct + 1]),
                ev=(f"vX2_{gc}" if ct == 1 else None))

        wt("scalar", f"vX2_{gc}")
        for ct in range(2):
            op("scalar", lambda e, ct=ct: e.activation(
                junk_s[:, :], x2_tm[:, D * ct:D * (ct + 1)], ACT.Square,
                scale=1.0 / 32.0, accum_out=st2[:, 8 + ct:9 + ct]))
        _ln_algebra(st2, 0, 4, 8, f"sS3_{gc}")

        wt("vector", f"sS3_{gc}")
        op("vector", lambda e: e.reciprocal(st2[:, 6:8], st2[:, 8:10]),
           ev=f"vR3_{gc}")
        wt("vector", f"vR3_{gc}")
        wt("vector", f"T3_{gc - 1}")
        for ct in range(2):
            op("vector", lambda e, ct=ct: e.scalar_tensor_tensor(
                h2_tm[:, D * ct:D * (ct + 1)], x2_tm[:, D * ct:D * (ct + 1)],
                st2[:, 4 + ct:5 + ct], rep["cb1"][:, :], OP.subtract, OP.mult))
        for ct in range(2):
            op("vector", lambda e, ct=ct: e.scalar_tensor_tensor(
                h2_tm[:, D * ct:D * (ct + 1)], h2_tm[:, D * ct:D * (ct + 1)],
                st2[:, 6 + ct:7 + ct], rep["sb_"][:, :], OP.mult, OP.add),
                ev=(f"vH2_{gc}" if ct == 1 else None))

        wt("sync", f"vH2_{gc}")
        wt("sync", f"tS_{gc - 1}")
        for ct in range(2):
            dma("sync", f"q3{gc % 2}", h2fv[:, :, 128 * ct:128 * (ct + 1)],
                h2_tm[:, D * ct:D * (ct + 1)], transpose=True,
                ev=(f"T3_{gc}" if ct == 1 else None))

    def B_tail(gc):
        s, c = divmod(gc, NCH)

        def da_mms(e):
            last = None
            for hh in range(2):
                for ct in range(2):
                    last = e.matmul(pda[0:R, 512 * hh:512 * (hh + 1)],
                                    pin_sb[:, R * ct:R * (ct + 1)],
                                    v_sb[:, 1024 * ct + 512 * hh:1024 * ct + 512 * (hh + 1)],
                                    start=(ct == 0), stop=(ct == 1),
                                    skip_group_check=True)
            return last
        op("tensor", da_mms, ev=f"tDMM_{gc}")

        wt("vector", f"tDMM_{gc}")
        wt("vector", f"tDB_{gc}")
        op("vector", lambda e: e.tensor_add(accAT[:, :], accAT[:, :], pda[0:R, 0:D]))
        op("vector", lambda e: e.tensor_add(accBT[:, :], accBT[:, :], pdb[:, :]),
           ev=f"vACC_{gc}")
        if c == NCH - 1 and s + 1 < NSEQ:
            op("vector", lambda e: e.memset(accAT[:, :], 0.0))
            op("vector", lambda e: e.memset(accBT[:, :], 0.0), ev=f"vRST{s + 1}")

    def gp_norm_squares(gc):
        s, c = divmod(gc, NCH)
        wt("scalar", f"vACC_{gc - 1}")
        if c == 0:
            wt("scalar", f"vRST{s}")
        op("scalar", lambda e: e.activation(junk_s[0:R, :], accAT[:, :], ACT.Square,
                                            accum_out=statn[0:R, 0:1]))
        op("scalar", lambda e: e.activation(junk_s[:, 0:128], accBT[:, :], ACT.Square,
                                            accum_out=statn[:, 1:2]), ev=f"sQF_{gc}")
        wt("scalar", f"sQF_{gc}")
        op("scalar", lambda e: e.copy(statnb[0:R, 0:1], statn[0:R, 0:1]))
        op("scalar", lambda e: e.copy(statnb[:, 1:2], statn[:, 1:2]),
           ev=f"gNRM_{gc}")

    def B_ffn(gc):
        s, c = divmod(gc, NCH)
        o2 = gc % 2
        h2fv = v3(h2fm[:, :], CH)

        wt("tensor", f"T3_{gc}")
        wt("tensor", f"vPE_{gc + 1}")
        if gc + 1 >= NSEQ * NCH:
            wt("tensor", f"vX2_{gc}")
        wt("tensor", "L_win")
        wt("tensor", "L_bdext")
        wt("tensor", "L_bd2")
        wt("tensor", "L_woutext")

        def s_mms(e):
            last = None
            for kt in range(KT):
                last = e.matmul(psm[0:M, 64:64 + CH], win_v[:, kt, :],
                                h2fv[:, kt, 0:CH], start=(kt == 0),
                                stop=(kt == KT - 1), skip_group_check=True)
            return last
        op("tensor", s_mms, ev=f"tS_{gc}")

        wt("scalar", f"tS_{gc}")
        wt("scalar", "L_binc")
        op("scalar", lambda e: e.activation(sext[0:M, :], psm[0:M, 64:64 + CH],
                                            ACT.Identity, bias=binc_sb[:, :]),
           ev=f"sSEV_{gc}")

        wt("tensor", f"sSEV_{gc}")
        wt("tensor", f"sVEV_{gc + 1}")

        def t_mms(e):
            last = None
            for g8 in range(8):
                last = e.matmul(pbig[:, 256 * g8:256 * (g8 + 1)],
                                bdext_sb[0:M + 1, 128 * g8:128 * (g8 + 1)],
                                sext[0:M + 1, :], start=True, stop=True,
                                skip_group_check=True)
            return last
        op("tensor", t_mms, ev=f"tT_{gc}")

        wt("scalar", f"tT_{gc}")
        op("scalar", lambda e: e.activation(t_sb[:, :], pbig[:, 0:2048], ACT.Relu),
           ev=f"sRLV_{gc}")

        wt("tensor", f"sRLV_{gc}")

        def g_mms(e):
            last = None
            for g8 in range(8):
                last = e.matmul(psm[0:M, 64:64 + CH], bd2_v[:, g8, :],
                                t_sb[:, 256 * g8:256 * (g8 + 1)],
                                start=(g8 == 0), stop=(g8 == 7),
                                skip_group_check=True)
            return last
        op("tensor", g_mms, ev=f"tG_{gc}")

        wt("scalar", f"tG_{gc}")
        wt("scalar", "L_b2c")
        op("scalar", lambda e: e.activation(gext[0:M, :], psm[0:M, 64:64 + CH],
                                            ACT.Identity, bias=b2c_sb[:, :]),
           ev=f"sGEV_{gc}")

        wt("tensor", f"sGEV_{gc}")
        wt("tensor", f"vWOF{s}")

        def out_mms(e):
            last = None
            for ct in range(2):
                for oh in range(2):
                    last = e.matmul(pbig[:, 1024 * ct + 512 * oh:1024 * ct + 512 * (oh + 1)],
                                    gext[0:M + 1, 128 * ct:128 * (ct + 1)],
                                    wouteff[0:M + 1, 512 * oh:512 * (oh + 1)],
                                    start=True, stop=True, skip_group_check=True)
            return last
        op("tensor", out_mms, ev=f"tFMM_{gc}")

        wt("vector", f"tFMM_{gc}")
        if c == NCH - 1 and s + 1 < NSEQ:
            op("vector", lambda e, s=s: e.tensor_tensor(
                wouteff[:, :], woutext_sb[:, :], reps[s + 1]["gb"][0:M + 1, :],
                OP.mult), ev=f"vWOF{s + 1}")
        wt("vector", f"Dst_{gc - 2}")
        ob = obuf[o2]
        for ct in range(2):
            op("vector", lambda e, ct=ct: e.scalar_tensor_tensor(
                ob[:, D * ct:D * (ct + 1)], pbig[:, 1024 * ct:1024 * (ct + 1)],
                1.0, x2_tm[:, D * ct:D * (ct + 1)], OP.mult, OP.add),
                ev=(f"vOUT_{gc}" if ct == 1 else None))

        wt("sync", f"vOUT_{gc}")
        rows = (s * NCH + c) * CH
        dma("sync", f"st{gc % 2}",
            dout[rows:rows + CH, :].rearrange("(a p) c -> p a c", p=128),
            v3(ob[:, :], D)[:, :, :], ev=f"Dst_{gc}")

    NG = NSEQ * NCH
    for g0 in range(min(2, NG)):
        A_load(g0)
        A_ln(g0)
        A_tr(g0)
    if NG > 2:
        A_load(2)
    gp_norm_squares(0)
    B_V(0)
    if NG > 2:
        A_ln(2)
    B_base(0)
    if NG > 2:
        A_tr(2)
    for j in range(NG):
        B_tail(j)
        if j + 3 < NG:
            A_load(j + 3)
        if j + 1 < NG:
            gp_norm_squares(j + 1)
            B_V(j + 1)
        if j + 3 < NG:
            A_ln(j + 3)
        B_ffn(j)
        if j + 1 < NG:
            B_base(j + 1)
        if j + 3 < NG:
            A_tr(j + 3)

    waitval("sync", "st0", cnt["st0"])
    waitval("sync", "st1", cnt["st1"])

    with nc.Block() as block:
        @block.tensor
        def _(e):
            for f in prog["tensor"]:
                f(e)

        @block.vector
        def _(e):
            for f in prog["vector"]:
                f(e)

        @block.scalar
        def _(e):
            for f in prog["scalar"]:
                f(e)

        @block.sync
        def _(e):
            for f in prog["sync"]:
                f(e)

        @block.gpsimd
        def _(e):
            for f in prog["gpsimd"]:
                f(e)

    return nc


def kernel(**inputs):
    ac = np.ascontiguousarray
    x = inputs["L0_attn_x"].astype(np.float32, copy=False)
    Wb = inputs["L0_attn_Wb"]
    Wp = inputs["L0_attn_Wp"]
    inA = inputs["L0_attn_init_A"]
    inB = inputs["L0_attn_init_B"]
    gamma = inputs["L0_attn_ln_gamma"]
    beta = inputs["L0_attn_ln_beta"]
    Win = inputs["L0_ffn_Win"]
    bin_ = inputs["L0_ffn_bin_"]
    W1 = inputs["L0_ffn_W1"]
    b1 = inputs["L0_ffn_b1"]
    W2 = inputs["L0_ffn_W2"]
    b2 = inputs["L0_ffn_b2"]
    Wout = inputs["L0_ffn_Wout"]
    bout = inputs["L0_ffn_bout"]
    cmod = inputs["L0_c"]
    wmod = inputs["L0_wmod"]

    bdext = np.zeros((M + 1, D), dtype=np.float32)
    bd2 = np.zeros((128, 8 * M), dtype=np.float32)
    for m in range(M):
        bdext[m, 16 * m:16 * m + 16] = W1[m]
        bdext[M, 16 * m:16 * m + 16] = b1[m]
        bd2[16 * (m % 8):16 * (m % 8) + 16, 64 * (m // 8) + m] = W2[m]
    shared = {
        "wbT": ac(np.asarray(Wb).T), "wpT": ac(np.asarray(Wp).T),
        "wp": ac(np.asarray(Wp)),
        "inA": ac(np.asarray(inA)), "inAT": ac(np.asarray(inA).T),
        "inBT": ac(np.asarray(inB).T),
        "gamma": ac(np.asarray(gamma).reshape(1, D)),
        "beta": ac(np.asarray(beta).reshape(1, D)),
        "win": ac(np.asarray(Win)),
        "binc": ac(np.asarray(bin_).reshape(M, 1)),
        "b2c": ac(np.asarray(b2).reshape(M, 1)),
        "bdext": bdext, "bd2": bd2,
        "woutext": ac(np.concatenate([np.asarray(Wout),
                                      np.asarray(bout).reshape(1, D)], axis=0)),
        "wmod": ac(np.asarray(wmod)),
    }
    nc = build_core()
    in_maps = []
    for i in range(NCORE):
        m = dict(shared)
        m["x"] = ac(np.asarray(x[2 * i:2 * i + 2]).reshape(NSEQ * NCH * CH, D))
        m["cvecT"] = ac(np.asarray(cmod[2 * i:2 * i + 2]).T)
        in_maps.append(m)
    res = run_bass_kernel_spmd(nc, in_maps, core_ids=list(range(NCORE)))
    out = np.concatenate(
        [r["out"].reshape(NSEQ, NCH * CH, D) for r in res.results], axis=0)
    return out
